# version 08
# baseline (speedup 1.0000x reference)
import contextlib
import numpy as np
import ml_dtypes
import concourse.bass as bass
import concourse.mybir as mybir
from concourse.bass_utils import run_bass_kernel_spmd

F32 = mybir.dt.float32
BF16 = mybir.dt.bfloat16
I32 = mybir.dt.int32
AF = mybir.ActivationFunctionType
ALU = mybir.AluOpType
AX = mybir.AxisListType
BF = ml_dtypes.bfloat16

S = 4096
DM = 2048
H = 16
HD = 128
OWN = 2048
HID = 5632
NHT = 44
NEG = -1e30
QSCALE = float(HD ** -0.5)

ENGINES = ("sync", "tensor", "vector", "scalar", "gpsimd")
DMA_ENG = ("sync", "gpsimd")


class Buf:
    __slots__ = ("w", "r")

    def __init__(self):
        self.w = None
        self.r = []


def _merge_floors(a, b):
    d = {}
    for (s, c) in list(a) + list(b):
        if d.get(s, 0) < c:
            d[s] = c
    return list(d.items())


class Prog:

    def __init__(self, nc, stack):
        self.nc = nc
        self.stack = stack
        self.items = {e: [] for e in ENGINES}
        self.sems = {e: [] for e in ENGINES}
        self.cnt = {e: 0 for e in ENGINES}
        self.hw = {e: {} for e in ENGINES}
        self.floors = {e: [] for e in ENGINES}
        for e in ENGINES:
            self._new_sem(e)

    def _new_sem(self, e):
        h = self.stack.enter_context(
            self.nc.semaphore(f"s_{e}_{len(self.sems[e])}"))
        self.sems[e].append(h)
        self.cnt[e] = 0

    def ins(self, eng, fn, reads=(), writes=()):
        pads = set()
        for b in list(reads) + list(writes):
            toks = ([b.w] if b.w is not None else []) + list(b.r)
            for t in toks:
                pe = t[0]
                if pe in ("vector", "tensor") and pe != eng \
                        and t[1] is self.sems[pe][-1] \
                        and t[2] >= self.cnt[pe]:
                    pads.add(pe)
        for pe in pads:
            self._drain_nop(pe)
        if eng == "scalar":
            fn = (lambda e, f=fn: (f(e), e.drain())[1])
        inc = 16 if eng in DMA_ENG else 1
        sem = self.sems[eng][-1]
        tok = (eng, sem, self.cnt[eng] + inc)
        deps = {}

        def add(t):
            if t is not None and t[0] != eng:
                c = t[2] + 1 if (t[0] in ("vector", "tensor")) else t[2]
                if deps.get(t[1], -1) < c:
                    deps[t[1]] = c
        for b in reads:
            add(b.w)
            b.r.append(tok)
        for b in writes:
            add(b.w)
            for t in b.r:
                add(t)
            b.w = tok
            b.r = []
        for (s, c) in self.floors[eng]:
            if deps.get(s, -1) < c:
                deps[s] = c
        self.floors[eng] = []
        hw = self.hw[eng]
        waits = []
        for s, c in deps.items():
            if hw.get(s, 0) < c:
                waits.append((s, c))
                hw[s] = c
        self.items[eng].append((fn, waits, sem, inc))
        self.cnt[eng] += inc
        return tok

    def _drain_nop(self, eng):
        sem = self.sems[eng][-1]
        if eng == "vector":
            fn = (lambda e: e.engine_nop())
        else:
            fn = (lambda e: e.drain())
        self.items[eng].append((fn, [], sem, 1))
        self.cnt[eng] += 1

    def _vector_nop(self):
        self._drain_nop("vector")

    def fence(self, eng):
        if self.cnt[eng] > 0:
            self.floors[eng] = _merge_floors(
                self.floors[eng], [(self.sems[eng][-1], self.cnt[eng])])

    def barrier(self):
        if self.cnt["vector"] > 0:
            self._vector_nop()
        snap = [(e, self.sems[e][-1], self.cnt[e]) for e in ENGINES
                if self.cnt[e] > 0]
        for e in ENGINES:
            fl = [(s, c) for (f, s, c) in snap if f != e]
            self.floors[e] = _merge_floors(self.floors[e], fl)
        for e in ENGINES:
            if self.cnt[e] > 0:
                self._new_sem(e)

    def finish(self):
        self._vector_nop()
        snap = [(self.sems[e][-1], self.cnt[e]) for e in ENGINES
                if self.cnt[e] > 0 and e != "vector"]
        self.floors["vector"] = _merge_floors(self.floors["vector"], snap)
        self.ins("vector", lambda e: e.engine_nop())

    def emit(self):
        with self.nc.Block() as blk:
            for e in ENGINES:
                if not self.items[e]:
                    continue

                def body(engobj, items=self.items[e]):
                    for fn, waits, sem, inc in items:
                        for (s, c) in waits:
                            engobj.wait_ge(s, c)
                        fn(engobj).then_inc(sem, inc)

                getattr(blk, e)(body)


def build(debug=()):
    nc = bass.Bass("TRN2", target_bir_lowering=False)
    stack = contextlib.ExitStack()

    xf = nc.dram_tensor("xf", [16, 128, S], F32, kind="ExternalInput").ap()
    xo = nc.dram_tensor("xo", [DM, OWN], F32, kind="ExternalInput").ap()
    wq = nc.dram_tensor("wq", [DM, DM], F32, kind="ExternalInput").ap()
    wk = nc.dram_tensor("wk", [DM, DM], F32, kind="ExternalInput").ap()
    wv = nc.dram_tensor("wv", [DM, DM], F32, kind="ExternalInput").ap()
    wo = nc.dram_tensor("wo", [DM, DM], F32, kind="ExternalInput").ap()
    wgT = nc.dram_tensor("wgT", [DM, HID], I32, kind="ExternalInput").ap()
    wuT = nc.dram_tensor("wuT", [DM, HID], I32, kind="ExternalInput").ap()
    wdT = nc.dram_tensor("wdT", [HID, DM], I32, kind="ExternalInput").ap()
    sg = nc.dram_tensor("sg", [128, NHT], F32, kind="ExternalInput").ap()
    su = nc.dram_tensor("su", [128, NHT], F32, kind="ExternalInput").ap()
    sd = nc.dram_tensor("sd", [128, 16], F32, kind="ExternalInput").ap()
    selfval_d = nc.dram_tensor("selfval", [128, 1024], BF16, kind="ExternalInput").ap()
    npval_d = nc.dram_tensor("npval", [128, 1024], BF16, kind="ExternalInput").ap()
    npbias_d = nc.dram_tensor("npbias", [128, 1024], BF16, kind="ExternalInput").ap()
    mA_d = nc.dram_tensor("mA", [128, 2048], BF16, kind="ExternalInput").ap()
    mB_d = nc.dram_tensor("mB", [128, 2048], BF16, kind="ExternalInput").ap()
    emat_d = nc.dram_tensor("emat", [64, S], BF16, kind="ExternalInput").ap()
    id_d = nc.dram_tensor("id128", [128, 128], BF16, kind="ExternalInput").ap()
    out_d = nc.dram_tensor("out", [DM, OWN], F32, kind="ExternalOutput").ap()

    def K(n):
        return "ExternalOutput" if n in debug else "Internal"
    xnf_d = nc.dram_tensor("xnf_s", [16, 128, S], BF16, kind=K("xnf_s")).ap()
    xno_d = nc.dram_tensor("xno_s", [16, 128, OWN], BF16, kind=K("xno_s")).ap()
    k_d = nc.dram_tensor("k_s", [H, 128, S], BF16, kind=K("k_s")).ap()
    v_d = nc.dram_tensor("v_s", [H, 32, 128, 128], BF16, kind=K("v_s")).ap()
    q_d = nc.dram_tensor("q_s", [H, 128, OWN], BF16, kind=K("q_s")).ap()
    o_d = nc.dram_tensor("o_s", [H, 128, OWN], BF16, kind=K("o_s")).ap()
    s_d = nc.dram_tensor("s_s", [16, 128, OWN], BF16, kind=K("s_s")).ap()
    act_d = nc.dram_tensor("act_s", [NHT, 128, 1024], BF16, kind=K("act_s")).ap()
    wdb_d = nc.dram_tensor("wdb_s", [NHT, 128, DM], BF16, kind=K("wdb_s")).ap()
    if "dbg" in debug:
        bt_dbg = nc.dram_tensor("bt_dbg", [H, 64, 2048], BF16,
                                kind="ExternalOutput").ap()
        g2_dbg = nc.dram_tensor("g2_dbg", [H, 128, 1024], F32,
                                kind="ExternalOutput").ap()
        t8_dbg = nc.dram_tensor("t8_dbg", [H, 128, 128], F32,
                                kind="ExternalOutput").ap()
        bqa_dbg = nc.dram_tensor("bqa_dbg", [H, 128, 1024], F32,
                                 kind="ExternalOutput").ap()
        id_dbg = nc.dram_tensor("id_dbg", [128, 128], BF16,
                                kind="ExternalOutput").ap()
        em_dbg = nc.dram_tensor("em_dbg", [64, S], BF16,
                                kind="ExternalOutput").ap()
        bq_dbg = nc.dram_tensor("bq_dbg", [H, 128, 1024], BF16,
                                kind="ExternalOutput").ap()

    A = nc.alloc_sbuf_tensor
    wsb = A("wsb", [128, 8 * 2048], BF16)
    oba = A("oba", [128, 16384], BF16)
    xna = A("xna", [128, 16384], BF16)
    fa = A("fa", [128, 4096], F32)
    rows = A("rows", [1, 1600], F32)
    rows_b = A("rows_b", [1, 1024], BF16)
    selfval = A("selfval_t", [128, 1024], BF16)
    npval = A("npval_t", [128, 1024], BF16)
    npbias = A("npbias_t", [128, 1024], BF16)
    mA = A("mA_t", [128, 4 * 512], BF16)
    mB = A("mB_t", [128, 4 * 512], BF16)
    emat = A("emat_t", [64, S], BF16)
    id128 = A("id128_t", [128, 128], BF16)
    sgc = A("sgc_t", [128, NHT], F32)
    suc = A("suc_t", [128, NHT], F32)
    sdc = A("sdc_t", [128, 16], F32)
    onescol = A("onescol", [128, 1], BF16).ap()
    cbeps = A("cbeps", [1, 1], F32).ap()
    cbtiny = A("cbtiny", [128, 1], F32).ap()
    onesrow_f = A("onesrow_f", [1, 128], F32).ap()
    m128row = A("m128row", [1, 128], BF16).ap()
    biasT = A("biasT", [64, 2048], BF16)
    t8 = A("t8", [128, 128], F32)
    bcols = A("bcols", [128, 64], F32)
    kgbt = A("kgbt", [128, 64], BF16)
    kst = A("kst", [128, 1024], BF16)
    pb = A("pbuf", [128, 1024], BF16)
    obuf = A("obuf", [128, 1024], BF16)
    rbuf = A("rbuf", [128, 512], F32)
    g2a = A("g2a", [128, 1024], F32)
    sela = A("sela", [128, 1024], F32)
    bqa = A("bqa", [128, 1024], F32)
    bqb = A("bqb", [128, 1024], BF16)
    outt = A("outt", [128, 1024], F32)
    gtj = A("gtj", [128, 1024], BF16)

    PS = [nc.alloc_psum_tensor(f"ps{i}", [128, 512], F32).ap() for i in range(7)]
    PST = nc.alloc_psum_tensor("ps7", [128, 1024], BF16).ap()

    p = Prog(nc, stack)
    B_ = {}

    def T(name):
        if name not in B_:
            B_[name] = Buf()
        return B_[name]

    def dma(eng, out_ap, in_ap, reads=(), writes=()):
        p.ins(eng, lambda e, o=out_ap, i=in_ap: e.dma_start(out=o, in_=i),
              reads=reads, writes=writes)

    dma("sync", selfval[:], selfval_d, writes=[T("selfval")])
    dma("sync", npval[:], npval_d, writes=[T("npval")])
    dma("sync", npbias[:], npbias_d, writes=[T("npbias")])
    dma("sync", mA[:], mA_d, writes=[T("mA")])
    dma("sync", mB[:], mB_d, writes=[T("mB")])
    dma("sync", emat[:], emat_d, writes=[T("emat")])
    dma("sync", id128[:], id_d, writes=[T("id128")])
    dma("sync", sgc[:], sg, writes=[T("sgc")])
    dma("sync", suc[:], su, writes=[T("suc")])
    dma("sync", sdc[:], sd, writes=[T("sdc")])
    p.ins("vector", lambda e: e.memset(onescol[:], 1.0), writes=[T("ones")])
    p.ins("vector", lambda e: e.memset(cbeps[:], 1e-5), writes=[T("ones")])
    p.ins("vector", lambda e: e.memset(cbtiny[:], 1e-30), writes=[T("ones")])
    p.ins("vector", lambda e: e.memset(onesrow_f[:], 1.0), writes=[T("ones")])
    p.ins("vector", lambda e: e.memset(m128row[:], -128.0), writes=[T("ones")])

    for j in range(NHT):
        buf = oba[:, (j % 2) * 2048:(j % 2) * 2048 + 2048]
        tb = T(f"wdbounce{j % 2}")
        p.ins("gpsimd", lambda e, o=buf, i=wdT[128 * j:128 * (j + 1), :]:
              e.dma_start(out=o, in_=i), reads=[tb], writes=[tb])
        dma("sync", wdb_d[j], buf, reads=[tb], writes=[T("wdb_dram")])
    p.barrier()

    GT = 256

    def emit_ln(src_ap, total_t, dst_ap, dst_tok):
        xA = fa[:, 0:GT * 16].rearrange("p (c t) -> p c t", t=GT)
        xb = oba[:, 0:GT * 16].rearrange("p (c t) -> p c t", t=GT)
        sq = oba[:, GT * 16:GT * 32].rearrange("p (c t) -> p c t", t=GT)
        xnb = oba[:, GT * 32:GT * 48].rearrange("p (c t) -> p c t", t=GT)
        tmp = rbuf[:, 0:GT]
        m_r = rows[0:1, 0:GT]
        msq = rows[0:1, GT:2 * GT]
        var = rows[0:1, 2 * GT:3 * GT]
        lnv = rows[0:1, 3 * GT:4 * GT]
        rstd = rows[0:1, 4 * GT:5 * GT]
        mrs = rows[0:1, 5 * GT:6 * GT]
        for g in range(total_t // GT):
            t0 = g * GT
            for dc in range(16):
                dma("sync", xA[:, dc, :], src_ap[dc, :, t0:t0 + GT],
                    writes=[T("ln_xA")])
            p.ins("vector", lambda e: e.tensor_copy(oba[:, 0:GT * 16],
                  fa[:, 0:GT * 16]),
                  reads=[T("ln_xA")], writes=[T("ln_xb")])
            p.ins("vector", lambda e: e.tensor_tensor(oba[:, GT * 16:GT * 32],
                  oba[:, 0:GT * 16], oba[:, 0:GT * 16], ALU.mult),
                  reads=[T("ln_xb")], writes=[T("ln_sq")])
            for dc in range(16):
                p.ins("tensor", lambda e, r=xb[:, dc, :], s=(dc == 0),
                      t=(dc == 15):
                      e.matmul(PS[4][0:1, 0:GT], onescol[:], r, start=s, stop=t),
                      reads=[T("ln_xb"), T("ones")], writes=[T("ps4")])
            for dc in range(16):
                p.ins("tensor", lambda e, r=sq[:, dc, :], s=(dc == 0),
                      t=(dc == 15):
                      e.matmul(PS[5][0:1, 0:GT], onescol[:], r, start=s, stop=t),
                      reads=[T("ln_sq"), T("ones")], writes=[T("ps5")])
            p.ins("scalar", lambda e: e.activation(m_r, PS[4][0:1, 0:GT],
                  AF.Copy, scale=1.0 / DM),
                  reads=[T("ps4")], writes=[T("ln_rows")])
            p.ins("scalar", lambda e: e.activation(msq, PS[4][0:1, 0:GT],
                  AF.Square, scale=1.0 / DM),
                  reads=[T("ps4")], writes=[T("ln_rows")])
            p.ins("vector", lambda e: e.scalar_tensor_tensor(var,
                  PS[5][0:1, 0:GT], 1.0 / DM, msq, ALU.mult, ALU.subtract),
                  reads=[T("ps5"), T("ln_rows")], writes=[T("ln_rows")])
            p.ins("scalar", lambda e: e.activation(lnv, var, AF.Ln, bias=cbeps),
                  reads=[T("ln_rows"), T("ones")], writes=[T("ln_rows")])
            p.fence("scalar")
            p.ins("scalar", lambda e: e.activation(rstd, lnv, AF.Exp, scale=-0.5),
                  reads=[T("ln_rows")], writes=[T("ln_rows")])
            p.ins("vector", lambda e: e.scalar_tensor_tensor(mrs, m_r, -1.0,
                  rstd, ALU.mult, ALU.mult),
                  reads=[T("ln_rows")], writes=[T("ln_rows")])
            p.ins("tensor", lambda e: e.matmul(PS[2][:, 0:GT], onesrow_f, rstd,
                  start=True, stop=True),
                  reads=[T("ln_rows"), T("ones")], writes=[T("ps2")])
            p.ins("tensor", lambda e: e.matmul(PS[3][:, 0:GT], onesrow_f, mrs,
                  start=True, stop=True),
                  reads=[T("ln_rows"), T("ones")], writes=[T("ps3")])
            for dc in range(16):
                p.ins("vector", lambda e, i=xA[:, dc, :]:
                      e.tensor_tensor(tmp, i, PS[2][:, 0:GT], ALU.mult),
                      reads=[T("ln_xA"), T("ps2")], writes=[T("ln_tmp")])
                p.ins("vector", lambda e, o=xnb[:, dc, :]:
                      e.tensor_tensor(o, tmp, PS[3][:, 0:GT], ALU.add),
                      reads=[T("ln_tmp"), T("ps3")], writes=[T("ln_xnb")])
                dma("sync", dst_ap[dc, :, t0:t0 + GT], xnb[:, dc, :],
                    reads=[T("ln_xnb")], writes=[T(dst_tok)])

    emit_ln(xf, S, xnf_d, "xnf")
    p.barrier()
    xov = xo.rearrange("(c p) t -> c p t", p=128)
    emit_ln(xov, OWN, xno_d, "xno")
    p.barrier()

    wsb_v = wsb[:].rearrange("p (c n) -> p c n", n=1024)

    def load_w(w_ap, ch):
        for dc in range(16):
            p.ins("gpsimd", lambda e, o=wsb_v[:, dc, :],
                  i=w_ap[128 * dc:128 * (dc + 1), 1024 * ch:1024 * (ch + 1)]:
                  e.dma_start(out=o, in_=i),
                  reads=[T("wsb")], writes=[T("wsb")])

    xnc = xna[:].rearrange("p (u c n) -> p u c n", u=2, n=512)

    def load_xn(src_ap, buf, t0):
        for dc in range(16):
            dma("sync", xnc[:, buf, dc, :], src_ap[dc, :, t0:t0 + 512],
                writes=[T(f"xnc{buf}")])

    for ch in range(2):
        load_w(wk, ch)
        for tch in range(8):
            bu = tch % 2
            load_xn(xnf_d, bu, tch * 512)
            for ddl in range(8):
                dd = ch * 8 + ddl
                bk = ddl % 2
                for dc in range(16):
                    p.ins("tensor", lambda e, b=PS[bk],
                          l=wsb_v[:, dc, 128 * ddl:128 * (ddl + 1)],
                          r=xnc[:, bu, dc, :], s=(dc == 0), t=(dc == 15):
                          e.matmul(b, l, r, start=s, stop=t),
                          reads=[T("wsb"), T(f"xnc{bu}")], writes=[T(f"ps{bk}")])
                stg = kst[:, bk * 512:bk * 512 + 512]
                p.ins("vector", lambda e, o=stg, b=PS[bk]: e.tensor_copy(o, b),
                      reads=[T(f"ps{bk}")], writes=[T(f"kst{bk}")])
                dma("sync", k_d[dd, :, tch * 512:tch * 512 + 512], stg,
                    reads=[T(f"kst{bk}")], writes=[T("k_dram")])
        p.barrier()

    for ch in range(2):
        load_w(wv, ch)
        for tch in range(8):
            bu = tch % 2
            load_xn(xnf_d, bu, tch * 512)
            for ttl in range(4):
                tt = tch * 4 + ttl
                for dvl in range(2):
                    dvc = ch * 2 + dvl
                    bk = (ttl * 2 + dvl) % 2
                    for dc in range(16):
                        p.ins("tensor", lambda e, b=PS[bk],
                              l=xnc[:, bu, dc, 128 * ttl:128 * (ttl + 1)],
                              r=wsb_v[:, dc, 512 * dvl:512 * (dvl + 1)],
                              s=(dc == 0), t=(dc == 15):
                              e.matmul(b, l, r, start=s, stop=t),
                              reads=[T("wsb"), T(f"xnc{bu}")],
                              writes=[T(f"ps{bk}")])
                    stg = kst[:, bk * 512:bk * 512 + 512]
                    p.ins("vector", lambda e, o=stg, b=PS[bk]:
                          e.tensor_copy(o, b),
                          reads=[T(f"ps{bk}")], writes=[T(f"kst{bk}")])
                    for hh in range(4):
                        dma("sync", v_d[dvc * 4 + hh, tt],
                            stg[:, 128 * hh:128 * (hh + 1)],
                            reads=[T(f"kst{bk}")], writes=[T("v_dram")])
        p.barrier()

    for ch in range(2):
        load_w(wq, ch)
        for och in range(4):
            bu = och % 2
            load_xn(xno_d, bu, och * 512)
            for ddl in range(8):
                dd = ch * 8 + ddl
                bk = ddl % 2
                for dc in range(16):
                    p.ins("tensor", lambda e, b=PS[bk],
                          l=wsb_v[:, dc, 128 * ddl:128 * (ddl + 1)],
                          r=xnc[:, bu, dc, :], s=(dc == 0), t=(dc == 15):
                          e.matmul(b, l, r, start=s, stop=t),
                          reads=[T("wsb"), T(f"xnc{bu}")], writes=[T(f"ps{bk}")])
                stg = kst[:, bk * 512:bk * 512 + 512]
                p.ins("scalar", lambda e, o=stg, b=PS[bk]:
                      e.activation(o, b, AF.Copy, scale=QSCALE),
                      reads=[T(f"ps{bk}")], writes=[T(f"kst{bk}")])
                dma("sync", q_d[dd, :, och * 512:och * 512 + 512], stg,
                    reads=[T(f"kst{bk}")], writes=[T("q_dram")])
        p.barrier()

    kb = oba[:, 0:4096]
    vb = oba[:, 4096:8192]
    qb = oba[:, 8192:10240]
    sqq = oba[:, 10240:12288]
    sqk = oba[:, 12288:16384]
    kgf = fa[:, 0:64]

    def emit_sumpv(kt, nkt):
        bk = kt % 2
        pbuf = pb[:, 512 * bk:512 * (bk + 1)]
        p.ins("tensor", lambda e, b=PS[3][0:1, :], r=pbuf, s=(kt == 0),
              t=(kt == nkt - 1): e.matmul(b, onescol[:], r, start=s, stop=t),
              reads=[T(f"pb{bk}"), T("ones")], writes=[T("ps3")])
        p.ins("tensor", lambda e, l=vb[:, 128 * kt:128 * (kt + 1)],
              r=pbuf, s=(kt == 0), t=(kt == nkt - 1):
              e.matmul(PS[2][:], l, r, start=s, stop=t),
              reads=[T(f"pb{bk}"), T("vb")], writes=[T("ps2")])

    for h in range(H):
        dma("sync", kb, k_d[h], writes=[T("kb")])
        dma("sync", qb, q_d[h], writes=[T("qb")])
        for tt in range(32):
            dma("sync", vb[:, 128 * tt:128 * (tt + 1)], v_d[h, tt],
                writes=[T("vb")])
        p.ins("vector", lambda e: e.tensor_tensor(sqk, kb, kb, ALU.mult),
              reads=[T("kb")], writes=[T("sqk")])
        p.ins("vector", lambda e: e.tensor_tensor(sqq, qb, qb, ALU.mult),
              reads=[T("qb")], writes=[T("sqq")])
        p.ins("vector", lambda e: e.tensor_reduce(kgf,
              kb.rearrange("p (n w) -> p n w", w=64), AX.X, ALU.add),
              reads=[T("kb")], writes=[T("kgf")])
        p.fence("vector")
        p.ins("vector", lambda e: e.tensor_scalar(kgbt[:], kgf, 1.0 / 64, None,
              ALU.mult), reads=[T("kgf")], writes=[T("kgb")])
        for kc in range(8):
            bk = 4 + (kc % 2)
            p.ins("tensor", lambda e, b=PS[bk][0:1, :],
                  r=sqk[:, 512 * kc:512 * (kc + 1)]:
                  e.matmul(b, onescol[:], r, start=True, stop=True),
                  reads=[T("sqk"), T("ones")], writes=[T(f"ps{bk}")])
            p.ins("vector", lambda e, o=rows[0:1, 1024 + kc:1025 + kc],
                  b=PS[bk][0:1, :]: e.tensor_reduce(o, b, AX.X, ALU.max),
                  reads=[T(f"ps{bk}")], writes=[T("kmax8")])
        p.fence("vector")
        p.ins("vector", lambda e: e.tensor_reduce(rows[0:1, 1032:1033],
              rows[0:1, 1024:1032], AX.X, ALU.max),
              reads=[T("kmax8")], writes=[T("kmax")])
        p.fence("vector")
        p.ins("tensor", lambda e: e.matmul(PS[5][:, 0:1], onesrow_f,
              rows[0:1, 1032:1033], start=True, stop=True),
              reads=[T("kmax"), T("ones")], writes=[T("ps5")])
        p.ins("vector", lambda e: e.tensor_copy(bcols[:, 48:49], PS[5][:, 0:1]),
              reads=[T("ps5")], writes=[T("kmaxc")])
        for i in range(16):
            p.ins("tensor", lambda e, o=PS[5][:, 32 + i:33 + i],
                  l=sqq[:, 128 * i:128 * (i + 1)]:
                  e.matmul(o, l, onescol[:], start=True, stop=True),
                  reads=[T("sqq"), T("ones")], writes=[T("ps5")])
        p.ins("vector", lambda e: e.tensor_scalar(bcols[:, 0:16],
              PS[5][:, 32:48], bcols[:, 48:49], None, ALU.mult),
              reads=[T("ps5"), T("kmaxc")], writes=[T("bsq")])
        p.ins("scalar", lambda e: e.activation(bcols[:, 16:32], bcols[:, 0:16],
              AF.Ln, bias=cbtiny), reads=[T("bsq"), T("ones")], writes=[T("blog")])
        p.fence("scalar")
        p.ins("scalar", lambda e: e.activation(bcols[:, 0:16], bcols[:, 16:32],
              AF.Exp, scale=0.5), reads=[T("blog")], writes=[T("bcol")])
        p.ins("vector", lambda e: e.tensor_scalar(bcols[:, 16:32],
              bcols[:, 0:16], -1.02, 60.0, ALU.mult, ALU.add),
              reads=[T("bcol")], writes=[T("ccol")])
        p.fence("vector")
        p.ins("vector", lambda e: e.tensor_scalar(bcols[:, 16:32],
              bcols[:, 16:32], 0.0, None, ALU.min),
              reads=[T("ccol")], writes=[T("ccol")])
        p.fence("vector")
        for grp in range(2):
            for r in range(8):
                i = grp * 8 + r
                p.ins("tensor", lambda e, o=PS[6][:, 64 * r:64 * (r + 1)],
                      l=qb[:, 128 * i:128 * (i + 1)]:
                      e.matmul(o, l, kgbt[:], start=True, stop=True),
                      reads=[T("qb"), T("kgb")], writes=[T("ps6")])
            sl = slice(512 * grp, 512 * (grp + 1))
            p.ins("vector", lambda e, s=sl: e.tensor_tensor(g2a[:, s],
                  PS[6][:], npval[:, s], ALU.min),
                  reads=[T("ps6"), T("npval")], writes=[T("g2")])
            p.ins("vector", lambda e, s=sl: e.tensor_tensor(sela[:, s],
                  g2a[:, s], selfval[:, s], ALU.max),
                  reads=[T("g2"), T("selfval")], writes=[T("sela")])
        for i in range(16):
            p.ins("vector", lambda e, o=t8[:, 8 * i:8 * (i + 1)],
                  i_=sela[:, 64 * i:64 * (i + 1)]: e.max(o, i_),
                  reads=[T("sela")], writes=[T("t8")])
        p.fence("vector")
        for i in range(16):
            p.ins("vector", lambda e, o=g2a[:, 64 * i:64 * (i + 1)],
                  i_=sela[:, 64 * i:64 * (i + 1)], th=t8[:, 8 * i + 7:8 * i + 8]:
                  e.tensor_scalar(o, i_, th, None, ALU.is_ge),
                  reads=[T("sela"), T("t8")], writes=[T("g2")])
        for i in range(16):
            p.ins("vector", lambda e, o=bqa[:, 64 * i:64 * (i + 1)],
                  i_=g2a[:, 64 * i:64 * (i + 1)], c=bcols[:, 16 + i:17 + i]:
                  e.tensor_scalar(o, i_, c, None, ALU.mult),
                  reads=[T("g2"), T("ccol")], writes=[T("bq")])
            p.ins("vector", lambda e, o=sela[:, 64 * i:64 * (i + 1)],
                  i_=g2a[:, 64 * i:64 * (i + 1)]:
                  e.tensor_scalar(o, i_, 1e30, NEG, ALU.mult, ALU.add),
                  reads=[T("g2")], writes=[T("sela")])
        p.ins("vector", lambda e: e.tensor_tensor(g2a[:], sela[:], npbias[:],
              ALU.min), reads=[T("sela"), T("npbias")], writes=[T("g2")])
        p.ins("vector", lambda e: e.tensor_tensor(bqb[:], bqa[:], g2a[:],
              ALU.add), reads=[T("bq"), T("g2")], writes=[T("bqb")])
        for i in range(16):
            p.ins("tensor", lambda e, i_=bqb[:, 64 * i:64 * (i + 1)]:
                  e.transpose(PST[0:64, 0:128], i_, id128[:]),
                  reads=[T("bqb"), T("id128")], writes=[T("ps7")])
            p.ins("vector", lambda e, o=biasT[:, 128 * i:128 * (i + 1)]:
                  e.tensor_copy(o, PST[0:64, 0:128]),
                  reads=[T("ps7")], writes=[T("biasT")])
        if "dbg" in debug:
            if h == 1:
                dma("sync", id_dbg, id128[:], reads=[T("id128")],
                    writes=[T("dbgw")])
                dma("sync", em_dbg, emat[:], reads=[T("emat")],
                    writes=[T("dbgw")])
            dma("sync", bt_dbg[h], biasT[:], reads=[T("biasT")],
                writes=[T("dbgw")])
            dma("sync", g2_dbg[h], g2a[:], reads=[T("g2")], writes=[T("dbgw")])
            dma("sync", bq_dbg[h], bqb[:], reads=[T("bqb")], writes=[T("dbgw")])
            dma("sync", t8_dbg[h], t8[:], reads=[T("t8")], writes=[T("dbgw")])
            dma("sync", bqa_dbg[h], bqa[:], reads=[T("bq")], writes=[T("dbgw")])
        for qc in range(4):
            nkt = 8 * qc + 8
            qsl = slice(512 * qc, 512 * (qc + 1))
            prev = None
            for kt in range(nkt):
                bk = kt % 2
                p.ins("tensor", lambda e, b=PS[bk],
                      l=emat[:, 128 * kt:128 * (kt + 1)], r=biasT[:, qsl]:
                      e.matmul(b, l, r, start=True, stop=False),
                      reads=[T("emat"), T("biasT")], writes=[T(f"ps{bk}")])
                rel = kt - 8 * qc
                if 0 <= rel < 8 and rel % 2 == 0:
                    r_ = rel // 2
                    p.ins("tensor", lambda e, b=PS[bk],
                          r=mA[:, 512 * r_:512 * (r_ + 1)]:
                          e.matmul(b, id128[:], r, start=False, stop=False),
                          reads=[T("mA"), T("id128")], writes=[T(f"ps{bk}")])
                if 1 <= rel < 9 and (rel - 1) % 2 == 0:
                    r_ = (rel - 1) // 2
                    p.ins("tensor", lambda e, b=PS[bk],
                          r=mB[:, 512 * r_:512 * (r_ + 1)]:
                          e.matmul(b, id128[:], r, start=False, stop=False),
                          reads=[T("mB"), T("id128")], writes=[T(f"ps{bk}")])
                p.ins("tensor", lambda e, b=PS[bk],
                      l=kb[:, 128 * kt:128 * (kt + 1)], r=qb[:, qsl]:
                      e.matmul(b, l, r, start=False, stop=True),
                      reads=[T("kb"), T("qb")], writes=[T(f"ps{bk}")])
                if prev is not None:
                    emit_sumpv(prev, nkt)
                p.ins("scalar", lambda e, o=pb[:, 512 * bk:512 * (bk + 1)],
                      b=PS[bk]: e.activation(o, b, AF.Exp),
                      reads=[T(f"ps{bk}")], writes=[T(f"pb{bk}")])
                prev = kt
            emit_sumpv(prev, nkt)
            p.ins("vector", lambda e: e.reciprocal(rows[0:1, 0:512],
                  PS[3][0:1, :]), reads=[T("ps3")], writes=[T("rsum")])
            p.ins("tensor", lambda e: e.matmul(PS[4][:], onesrow_f,
                  rows[0:1, 0:512], start=True, stop=True),
                  reads=[T("rsum"), T("ones")], writes=[T("ps4")])
            p.ins("vector", lambda e: e.tensor_copy(rbuf[:], PS[4][:]),
                  reads=[T("ps4")], writes=[T("rbuf")])
            ob = obuf[:, 512 * (qc % 2):512 * (qc % 2) + 512]
            p.ins("vector", lambda e, o=ob: e.tensor_tensor(o, PS[2][:],
                  rbuf[:], ALU.mult),
                  reads=[T("ps2"), T("rbuf")], writes=[T(f"ob{qc % 2}")])
            dma("sync", o_d[h, :, 512 * qc:512 * (qc + 1)], ob,
                reads=[T(f"ob{qc % 2}")], writes=[T("o_dram")])
    p.barrier()

    och_v = xna[:].rearrange("p (c n) -> p c n", n=1024)
    for ch in range(2):
        load_w(wo, ch)
        for tch in range(4):
            bu = tch % 2
            for dc in range(16):
                dma("sync", och_v[:, dc, 512 * bu:512 * bu + 512],
                    o_d[dc, :, tch * 512:tch * 512 + 512],
                    writes=[T(f"oall{bu}")])
            for dol in range(8):
                dot = ch * 8 + dol
                bk = dol % 2
                for dc in range(16):
                    p.ins("tensor", lambda e, b=PS[bk],
                          l=wsb_v[:, dc, 128 * dol:128 * (dol + 1)],
                          r=och_v[:, dc, 512 * bu:512 * bu + 512],
                          s=(dc == 0), t=(dc == 15):
                          e.matmul(b, l, r, start=s, stop=t),
                          reads=[T("wsb"), T(f"oall{bu}")],
                          writes=[T(f"ps{bk}")])
                stg = kst[:, bk * 512:bk * 512 + 512]
                p.ins("vector", lambda e, o=stg, b=PS[bk]: e.tensor_copy(o, b),
                      reads=[T(f"ps{bk}")], writes=[T(f"kst{bk}")])
                dma("sync", s_d[dot, :, tch * 512:tch * 512 + 512], stg,
                    reads=[T(f"kst{bk}")], writes=[T("s_dram")])
        p.barrier()

    wgj = oba[:, 0:4096]
    wuj = oba[:, 4096:8192]
    actb_a = oba[:, 8192:10240]
    wdj = oba[:, 10240:11264]
    actj = oba[:, 11264:12288]
    stj = oba[:, 12288:13312]
    sgt = fa[:, 0:1024]
    xtj = fa[:, 1024:1536]
    xtj2 = fa[:, 1536:2048]
    xno2 = xna[:].rearrange("p (c n) -> p c n", n=1024)

    for tc in range(2):
        for dc in range(16):
            dma("sync", xno2[:, dc, :],
                xno_d[dc, :, tc * 1024:tc * 1024 + 1024], writes=[T("xno2")])
        for half in range(2):
            bk = 4 + half
            for dc in range(16):
                p.ins("tensor", lambda e, b=PS[bk][0:1, :],
                      r=xno2[:, dc, 512 * half:512 * half + 512],
                      s=(dc == 0), t=(dc == 15):
                      e.matmul(b, onescol[:], r, start=s, stop=t),
                      reads=[T("xno2"), T("ones")], writes=[T(f"ps{bk}")])
            p.ins("vector", lambda e, o=rows_b[0:1, 512 * half:512 * half + 512],
                  b=PS[bk][0:1, :]: e.tensor_copy(o, b),
                  reads=[T(f"ps{bk}")], writes=[T("crow")])
        for j in range(NHT):
            bu = j % 2
            wgv = wgj[:, 2048 * bu:2048 * bu + 2048].rearrange(
                "p (c n) -> p c n", n=128)
            wuv = wuj[:, 2048 * bu:2048 * bu + 2048].rearrange(
                "p (c n) -> p c n", n=128)
            actb = actb_a[:, 1024 * bu:1024 * bu + 1024]
            for h2 in range(2):
                p.ins("gpsimd", lambda e, o=wgv[:, 8 * h2:8 * (h2 + 1), :],
                      i=wgT[1024 * h2:1024 * (h2 + 1), 128 * j:128 * (j + 1)]
                      .rearrange("(c p) n -> p c n", p=128):
                      e.dma_start(out=o, in_=i),
                      reads=[T(f"wg{bu}")], writes=[T(f"wg{bu}")])
                p.ins("gpsimd", lambda e, o=wuv[:, 8 * h2:8 * (h2 + 1), :],
                      i=wuT[1024 * h2:1024 * (h2 + 1), 128 * j:128 * (j + 1)]
                      .rearrange("(c p) n -> p c n", p=128):
                      e.dma_start(out=o, in_=i),
                      reads=[T(f"wu{bu}")], writes=[T(f"wu{bu}")])
            for half in range(2):
                hsl = slice(512 * half, 512 * half + 512)
                gb = PS[half]
                ub = PS[2 + half]
                p.ins("tensor", lambda e, b=gb, r=rows_b[0:1, hsl]:
                      e.matmul(b, m128row, r, start=True, stop=False),
                      reads=[T("crow"), T("ones")], writes=[T(f"ps{half}")])
                for dc in range(16):
                    p.ins("tensor", lambda e, b=gb, l=wgv[:, dc, :],
                          r=xno2[:, dc, hsl], t=(dc == 15):
                          e.matmul(b, l, r, start=False, stop=t),
                          reads=[T(f"wg{bu}"), T("xno2")],
                          writes=[T(f"ps{half}")])
                p.ins("tensor", lambda e, b=ub, r=rows_b[0:1, hsl]:
                      e.matmul(b, m128row, r, start=True, stop=False),
                      reads=[T("crow"), T("ones")], writes=[T(f"ps{2 + half}")])
                for dc in range(16):
                    p.ins("tensor", lambda e, b=ub, l=wuv[:, dc, :],
                          r=xno2[:, dc, hsl], t=(dc == 15):
                          e.matmul(b, l, r, start=False, stop=t),
                          reads=[T(f"wu{bu}"), T("xno2")],
                          writes=[T(f"ps{2 + half}")])
                sgv = sgt[:, 512 * half:512 * half + 512]
                p.ins("scalar", lambda e, o=sgv, b=gb, sc=sgc[:, j:j + 1]:
                      e.activation(o, b, AF.Silu, scale=sc),
                      reads=[T(f"ps{half}"), T("sgc")], writes=[T(f"sg{half}")])
                p.ins("vector", lambda e, o=actb[:, hsl], b=ub,
                      sc=suc[:, j:j + 1], s2=sgv:
                      e.scalar_tensor_tensor(o, b, sc, s2, ALU.mult, ALU.mult),
                      reads=[T(f"ps{2 + half}"), T("suc"), T(f"sg{half}")],
                      writes=[T(f"actb{bu}")])
                p.ins("tensor", lambda e, b=PS[4 + half][0:1, :],
                      r=actb[:, hsl], s=(j == 0), t=(j == NHT - 1):
                      e.matmul(b, onescol[:], r, start=s, stop=t),
                      reads=[T(f"actb{bu}"), T("ones")],
                      writes=[T(f"ps{4 + half}")])
            dma("sync", act_d[j], actb, reads=[T(f"actb{bu}")],
                writes=[T("act_dram")])
        for half in range(2):
            p.ins("vector", lambda e, o=rows_b[0:1, 512 * half:512 * half + 512],
                  b=PS[4 + half][0:1, :]: e.tensor_copy(o, b),
                  reads=[T(f"ps{4 + half}")], writes=[T("carow")])
        p.barrier()
        for tsub in range(2):
            tpos = tc * 1024 + 512 * tsub
            for dg in range(4):
                for u in range(4):
                    p.ins("tensor", lambda e, b=PS[u],
                          r=rows_b[0:1, 512 * tsub:512 * tsub + 512]:
                          e.matmul(b, m128row, r, start=True, stop=False),
                          reads=[T("carow"), T("ones")], writes=[T(f"ps{u}")])
                for j in range(NHT):
                    bu = j % 2
                    aj = actj[:, 512 * bu:512 * bu + 512]
                    wj = wdj[:, 512 * bu:512 * bu + 512]
                    dma("sync", aj,
                        act_d[j, :, 512 * tsub:512 * tsub + 512],
                        reads=[T(f"actj{bu}")], writes=[T(f"actj{bu}")])
                    dma("sync", wj, wdb_d[j, :, 512 * dg:512 * dg + 512],
                        reads=[T(f"wdj{bu}")], writes=[T(f"wdj{bu}")])
                    for u in range(4):
                        p.ins("tensor", lambda e, b=PS[u],
                              l=wj[:, 128 * u:128 * (u + 1)], r=aj,
                              t=(j == NHT - 1):
                              e.matmul(b, l, r, start=False, stop=t),
                              reads=[T(f"wdj{bu}"), T(f"actj{bu}")],
                              writes=[T(f"ps{u}")])
                for u in range(4):
                    dot = dg * 4 + u
                    par = dot % 2
                    gt = gtj[:, 512 * par:512 * par + 512]
                    st_ = stj[:, 512 * par:512 * par + 512]
                    xt_ = xtj if par == 0 else xtj2
                    ot = outt[:, 512 * par:512 * par + 512]
                    p.ins("scalar", lambda e, o=gt, b=PS[u],
                          sc=sdc[:, dot:dot + 1]:
                          e.activation(o, b, AF.Silu, scale=sc),
                          reads=[T(f"ps{u}"), T("sdc")], writes=[T(f"gt{par}")])
                    dma("sync", st_, s_d[dot, :, tpos:tpos + 512],
                        reads=[T(f"stj{par}")], writes=[T(f"stj{par}")])
                    dma("sync", xt_,
                        xo[128 * dot:128 * (dot + 1), tpos:tpos + 512],
                        reads=[T(f"xtj{par}")], writes=[T(f"xtj{par}")])
                    p.ins("vector", lambda e, o=gt, s_=st_:
                          e.tensor_tensor(o, o, s_, ALU.mult),
                          reads=[T(f"gt{par}"), T(f"stj{par}")],
                          writes=[T(f"gt{par}")])
                    p.ins("vector", lambda e, o=ot, g=gt, x=xt_:
                          e.tensor_tensor(o, g, x, ALU.add),
                          reads=[T(f"gt{par}"), T(f"xtj{par}")],
                          writes=[T(f"outt{par}")])
                    dma("sync",
                        out_d[128 * dot:128 * (dot + 1), tpos:tpos + 512], ot,
                        reads=[T(f"outt{par}")], writes=[T("out_dram")])
        p.barrier()

    p.finish()
    p.emit()
    return nc, stack


_CACHE = {}


def _get_prog():
    if "nc" not in _CACHE:
        nc, stack = build()
        _CACHE["nc"] = nc
        _CACHE["stack"] = stack
    return _CACHE["nc"]


def _host_masks(parity):
    g = np.array([2 * i + parity for i in range(16)])
    P = g[:, None] * 128 + np.arange(128)[None, :]
    n = np.arange(64)[None, None, :]
    selfblk = (P[..., None] // 64)
    selfval = np.where(n == selfblk, 1e30, NEG)
    npval = np.where(P[..., None] < (n + 1) * 64, NEG, 1e30)
    npbias = np.where(n > selfblk, NEG, 0.0)

    def pack(m):
        return np.ascontiguousarray(
            m.transpose(1, 0, 2).reshape(128, 1024)).astype(BF)

    tri = np.where(np.arange(128)[:, None] > np.arange(128)[None, :],
                   NEG, 0.0).astype(np.float32)
    full = np.full((128, 128), NEG, dtype=np.float32)
    zero = np.zeros((128, 128), dtype=np.float32)
    tA = tri if parity == 0 else zero
    tB = full if parity == 0 else tri
    mA_ = np.zeros((4, 128, 512), np.float32)
    mB_ = np.zeros((4, 128, 512), np.float32)
    for r in range(4):
        mA_[r, :, 128 * r:128 * (r + 1)] = tA
        mB_[r, :, 128 * r:128 * (r + 1)] = tB
    return (pack(selfval), pack(npval), pack(npbias),
            np.ascontiguousarray(mA_.transpose(1, 0, 2).reshape(128, 2048)
                                 .astype(BF)),
            np.ascontiguousarray(mB_.transpose(1, 0, 2).reshape(128, 2048)
                                 .astype(BF)))


def kernel(L0_attn_hidden_states, L0_attn_Wq, L0_attn_Wk, L0_attn_Wv,
           L0_attn_Wo, L0_ffn_wq_gate, L0_ffn_s_gate, L0_ffn_wq_up,
           L0_ffn_s_up, L0_ffn_wq_down, L0_ffn_s_down):
    x = np.asarray(L0_attn_hidden_states, dtype=np.float32)
    shared = {
        "wq": np.ascontiguousarray(np.asarray(L0_attn_Wq, np.float32)),
        "wk": np.ascontiguousarray(np.asarray(L0_attn_Wk, np.float32)),
        "wv": np.ascontiguousarray(np.asarray(L0_attn_Wv, np.float32)),
        "wo": np.ascontiguousarray(np.asarray(L0_attn_Wo, np.float32)),
        "wgT": np.ascontiguousarray(np.asarray(L0_ffn_wq_gate, np.int32).T),
        "wuT": np.ascontiguousarray(np.asarray(L0_ffn_wq_up, np.int32).T),
        "wdT": np.ascontiguousarray(np.asarray(L0_ffn_wq_down, np.int32).T),
        "sg": np.ascontiguousarray(
            np.asarray(L0_ffn_s_gate, np.float32).reshape(NHT, 128).T),
        "su": np.ascontiguousarray(
            np.asarray(L0_ffn_s_up, np.float32).reshape(NHT, 128).T),
        "sd": np.ascontiguousarray(
            np.asarray(L0_ffn_s_down, np.float32).reshape(16, 128).T),
        "emat": np.ascontiguousarray(
            (np.arange(64)[:, None] == (np.arange(S)[None, :] // 64))
            .astype(BF)),
        "id128": np.eye(128).astype(BF),
    }
    in_maps = []
    own_idx_all = []
    for core in range(8):
        b, parity = core // 2, core % 2
        xT = np.ascontiguousarray(x[b].T)
        own = np.concatenate([np.arange(128 * (2 * i + parity),
                                        128 * (2 * i + parity) + 128)
                              for i in range(16)])
        own_idx_all.append(own)
        sv, npv, npb, mA_, mB_ = _host_masks(parity)
        m = dict(shared)
        m["xf"] = np.ascontiguousarray(xT.reshape(16, 128, S))
        m["xo"] = np.ascontiguousarray(xT[:, own])
        m["selfval"] = sv
        m["npval"] = npv
        m["npbias"] = npb
        m["mA"] = mA_
        m["mB"] = mB_
        in_maps.append(m)

    nc = _get_prog()
    res = run_bass_kernel_spmd(nc, in_maps, core_ids=list(range(8)))
    out = np.empty((4, S, DM), dtype=np.float32)
    for core in range(8):
        b = core // 2
        out[b, own_idx_all[core], :] = res.results[core]["out"].T
    return out


# version 10
# speedup vs baseline: 1.2260x; 1.2260x over previous
import contextlib
import numpy as np
import ml_dtypes
import concourse.bass as bass
import concourse.mybir as mybir
from concourse.bass_utils import run_bass_kernel_spmd

F32 = mybir.dt.float32
BF16 = mybir.dt.bfloat16
I32 = mybir.dt.int32
AF = mybir.ActivationFunctionType
ALU = mybir.AluOpType
AX = mybir.AxisListType
BF = ml_dtypes.bfloat16

S = 4096
DM = 2048
H = 16
HD = 128
OWN = 2048
HID = 5632
NHT = 44
NEG = -1e30
QSCALE = float(HD ** -0.5)

ENGINES = ("sync", "tensor", "vector", "scalar", "gpsimd")
DMA_ENG = ("sync", "gpsimd")


class Buf:
    __slots__ = ("w", "r")

    def __init__(self):
        self.w = None
        self.r = []


def _merge_floors(a, b):
    d = {}
    for (s, c) in list(a) + list(b):
        if d.get(s, 0) < c:
            d[s] = c
    return list(d.items())


class Prog:

    def __init__(self, nc, stack):
        self.nc = nc
        self.stack = stack
        self.items = {e: [] for e in ENGINES}
        self.sems = {e: [] for e in ENGINES}
        self.cnt = {e: 0 for e in ENGINES}
        self.hw = {e: {} for e in ENGINES}
        self.floors = {e: [] for e in ENGINES}
        for e in ENGINES:
            self._new_sem(e)

    def _new_sem(self, e):
        h = self.stack.enter_context(
            self.nc.semaphore(f"s_{e}_{len(self.sems[e])}"))
        self.sems[e].append(h)
        self.cnt[e] = 0

    def ins(self, eng, fn, reads=(), writes=()):
        need_pad = False
        for b in list(reads) + list(writes):
            if b.w is not None and b.w[0] == "vector" and eng != "vector"                     and b.w[1] is self.sems["vector"][-1]                     and b.w[2] >= self.cnt["vector"]:
                need_pad = True
            for t in b.r:
                if t[0] == "vector" and eng != "vector"                         and t[1] is self.sems["vector"][-1]                         and t[2] >= self.cnt["vector"]:
                    need_pad = True
        if need_pad:
            self._vector_nop()
        inc = 16 if eng in DMA_ENG else 1
        sem = self.sems[eng][-1]
        tok = (eng, sem, self.cnt[eng] + inc)
        deps = {}

        def add(t):
            if t is not None and t[0] != eng:
                c = t[2] + 1 if (t[0] == "vector") else t[2]
                if deps.get(t[1], -1) < c:
                    deps[t[1]] = c
        for b in reads:
            add(b.w)
            b.r.append(tok)
        for b in writes:
            add(b.w)
            for t in b.r:
                add(t)
            b.w = tok
            b.r = []
        for (s, c) in self.floors[eng]:
            if deps.get(s, -1) < c:
                deps[s] = c
        self.floors[eng] = []
        hw = self.hw[eng]
        waits = []
        for s, c in deps.items():
            if hw.get(s, 0) < c:
                waits.append((s, c))
                hw[s] = c
        self.items[eng].append((fn, waits, sem, inc))
        self.cnt[eng] += inc
        return tok

    def _vector_nop(self):
        sem = self.sems["vector"][-1]
        self.items["vector"].append(
            (lambda e: e.engine_nop(), [], sem, 1))
        self.cnt["vector"] += 1

    def fence(self, eng):
        if self.cnt[eng] > 0:
            self.floors[eng] = _merge_floors(
                self.floors[eng], [(self.sems[eng][-1], self.cnt[eng])])

    def barrier(self):
        if self.cnt["vector"] > 0:
            self._vector_nop()
        snap = [(e, self.sems[e][-1], self.cnt[e]) for e in ENGINES
                if self.cnt[e] > 0]
        for e in ENGINES:
            fl = [(s, c) for (f, s, c) in snap if f != e]
            self.floors[e] = _merge_floors(self.floors[e], fl)
        for e in ENGINES:
            if self.cnt[e] > 0:
                self._new_sem(e)

    def finish(self):
        self._vector_nop()
        snap = [(self.sems[e][-1], self.cnt[e]) for e in ENGINES
                if self.cnt[e] > 0 and e != "vector"]
        self.floors["vector"] = _merge_floors(self.floors["vector"], snap)
        self.ins("vector", lambda e: e.engine_nop())

    def emit(self):
        with self.nc.Block() as blk:
            for e in ENGINES:
                if not self.items[e]:
                    continue

                def body(engobj, items=self.items[e]):
                    for fn, waits, sem, inc in items:
                        for (s, c) in waits:
                            engobj.wait_ge(s, c)
                        fn(engobj).then_inc(sem, inc)

                getattr(blk, e)(body)


def build(debug=()):
    nc = bass.Bass("TRN2", target_bir_lowering=False)
    stack = contextlib.ExitStack()

    xf = nc.dram_tensor("xf", [16, 128, S], F32, kind="ExternalInput").ap()
    xo = nc.dram_tensor("xo", [DM, OWN], F32, kind="ExternalInput").ap()
    wq = nc.dram_tensor("wq", [DM, DM], F32, kind="ExternalInput").ap()
    wk = nc.dram_tensor("wk", [DM, DM], F32, kind="ExternalInput").ap()
    wv = nc.dram_tensor("wv", [DM, DM], F32, kind="ExternalInput").ap()
    wo = nc.dram_tensor("wo", [DM, DM], F32, kind="ExternalInput").ap()
    wgA = nc.dram_tensor("wgA", [NHT, 128, 2048], I32, kind="ExternalInput").ap()
    wuA = nc.dram_tensor("wuA", [NHT, 128, 2048], I32, kind="ExternalInput").ap()
    wdR = nc.dram_tensor("wdR", [NHT, 128, 2048], I32, kind="ExternalInput").ap()
    sg = nc.dram_tensor("sg", [128, NHT], F32, kind="ExternalInput").ap()
    su = nc.dram_tensor("su", [128, NHT], F32, kind="ExternalInput").ap()
    sd = nc.dram_tensor("sd", [128, 16], F32, kind="ExternalInput").ap()
    selfval_d = nc.dram_tensor("selfval", [128, 1024], BF16, kind="ExternalInput").ap()
    npval_d = nc.dram_tensor("npval", [128, 1024], BF16, kind="ExternalInput").ap()
    npbias_d = nc.dram_tensor("npbias", [128, 1024], BF16, kind="ExternalInput").ap()
    mA_d = nc.dram_tensor("mA", [128, 2048], BF16, kind="ExternalInput").ap()
    mB_d = nc.dram_tensor("mB", [128, 2048], BF16, kind="ExternalInput").ap()
    emat_d = nc.dram_tensor("emat", [64, S], BF16, kind="ExternalInput").ap()
    id_d = nc.dram_tensor("id128", [128, 128], BF16, kind="ExternalInput").ap()
    out_d = nc.dram_tensor("out", [DM, OWN], F32, kind="ExternalOutput").ap()

    def K(n):
        return "ExternalOutput" if n in debug else "Internal"
    xnf_d = nc.dram_tensor("xnf_s", [16, 128, S], BF16, kind=K("xnf_s")).ap()
    xno_d = nc.dram_tensor("xno_s", [16, 128, OWN], BF16, kind=K("xno_s")).ap()
    k_d = nc.dram_tensor("k_s", [H, 128, S], BF16, kind=K("k_s")).ap()
    v_d = nc.dram_tensor("v_s", [H, 32, 128, 128], BF16, kind=K("v_s")).ap()
    q_d = nc.dram_tensor("q_s", [H, 128, OWN], BF16, kind=K("q_s")).ap()
    o_d = nc.dram_tensor("o_s", [H, 128, OWN], BF16, kind=K("o_s")).ap()
    s_d = nc.dram_tensor("s_s", [16, 128, OWN], BF16, kind=K("s_s")).ap()
    act_d = nc.dram_tensor("act_s", [NHT, 128, 1024], BF16, kind=K("act_s")).ap()
    wdb_d = nc.dram_tensor("wdb_s", [NHT, 128, DM], BF16, kind=K("wdb_s")).ap()
    if "dbg" in debug:
        bt_dbg = nc.dram_tensor("bt_dbg", [H, 64, 2048], BF16,
                                kind="ExternalOutput").ap()
        g2_dbg = nc.dram_tensor("g2_dbg", [H, 128, 1024], F32,
                                kind="ExternalOutput").ap()
        t8_dbg = nc.dram_tensor("t8_dbg", [H, 128, 128], F32,
                                kind="ExternalOutput").ap()
        bqa_dbg = nc.dram_tensor("bqa_dbg", [H, 128, 1024], F32,
                                 kind="ExternalOutput").ap()
        id_dbg = nc.dram_tensor("id_dbg", [128, 128], BF16,
                                kind="ExternalOutput").ap()
        em_dbg = nc.dram_tensor("em_dbg", [64, S], BF16,
                                kind="ExternalOutput").ap()
        bq_dbg = nc.dram_tensor("bq_dbg", [H, 128, 1024], BF16,
                                kind="ExternalOutput").ap()

    A = nc.alloc_sbuf_tensor
    wsb = A("wsb", [128, 8 * 2048], BF16)
    oba = A("oba", [128, 16384], BF16)
    xna = A("xna", [128, 16384], BF16)
    fa = A("fa", [128, 4096], F32)
    rows = A("rows", [1, 3072], F32)
    rows_b = A("rows_b", [1, 1024], BF16)
    selfval = A("selfval_t", [128, 1024], BF16)
    npval = A("npval_t", [128, 1024], BF16)
    npbias = A("npbias_t", [128, 1024], BF16)
    mA = A("mA_t", [128, 4 * 512], BF16)
    mB = A("mB_t", [128, 4 * 512], BF16)
    emat = A("emat_t", [64, S], BF16)
    id128 = A("id128_t", [128, 128], BF16)
    sgc = A("sgc_t", [128, NHT], F32)
    suc = A("suc_t", [128, NHT], F32)
    sdc = A("sdc_t", [128, 16], F32)
    onescol = A("onescol", [128, 1], BF16).ap()
    onescol_f = A("onescol_f", [128, 1], F32).ap()
    cbeps = A("cbeps", [1, 1], F32).ap()
    cbtiny = A("cbtiny", [128, 1], F32).ap()
    onesrow_f = A("onesrow_f", [1, 128], F32).ap()
    m128row = A("m128row", [1, 128], BF16).ap()
    biasT = A("biasT", [64, 2048], BF16)
    t8 = A("t8", [128, 128], F32)
    bcols = A("bcols", [128, 64], F32)
    kgbt = A("kgbt", [128, 64], BF16)
    kst = A("kst", [128, 1024], BF16)
    pb = A("pbuf", [128, 1024], BF16)
    obuf = A("obuf", [128, 1024], BF16)
    rbuf = A("rbuf", [128, 512], F32)
    g2a = A("g2a", [128, 1024], F32)
    sela = A("sela", [128, 1024], F32)
    bqa = A("bqa", [128, 1024], F32)
    bqb = A("bqb", [128, 1024], BF16)
    outt = A("outt", [128, 2048], F32)
    gtj = A("gtj", [128, 2048], BF16)

    PS = [nc.alloc_psum_tensor(f"ps{i}", [128, 512], F32).ap() for i in range(7)]
    PST = nc.alloc_psum_tensor("ps7", [128, 1024], BF16).ap()

    p = Prog(nc, stack)
    B_ = {}

    def T(name):
        if name not in B_:
            B_[name] = Buf()
        return B_[name]

    def dma(eng, out_ap, in_ap, reads=(), writes=()):
        p.ins(eng, lambda e, o=out_ap, i=in_ap: e.dma_start(out=o, in_=i),
              reads=reads, writes=writes)

    dma("sync", selfval[:], selfval_d, writes=[T("selfval")])
    dma("sync", npval[:], npval_d, writes=[T("npval")])
    dma("sync", npbias[:], npbias_d, writes=[T("npbias")])
    dma("sync", mA[:], mA_d, writes=[T("mA")])
    dma("sync", mB[:], mB_d, writes=[T("mB")])
    dma("sync", emat[:], emat_d, writes=[T("emat")])
    dma("sync", id128[:], id_d, writes=[T("id128")])
    dma("sync", sgc[:], sg, writes=[T("sgc")])
    dma("sync", suc[:], su, writes=[T("suc")])
    dma("sync", sdc[:], sd, writes=[T("sdc")])
    p.ins("vector", lambda e: e.memset(onescol[:], 1.0), writes=[T("ones")])
    p.ins("vector", lambda e: e.memset(onescol_f[:], 1.0), writes=[T("ones")])
    p.ins("vector", lambda e: e.memset(cbeps[:], 1e-5), writes=[T("ones")])
    p.ins("vector", lambda e: e.memset(cbtiny[:], 1e-30), writes=[T("ones")])
    p.ins("vector", lambda e: e.memset(onesrow_f[:], 1.0), writes=[T("ones")])
    p.ins("vector", lambda e: e.memset(m128row[:], -128.0), writes=[T("ones")])

    for j in range(NHT):
        buf = oba[:, (j % 2) * 2048:(j % 2) * 2048 + 2048]
        tb = T(f"wdbounce{j % 2}")
        p.ins("gpsimd", lambda e, o=buf, i=wdR[j]:
              e.dma_start(out=o, in_=i), reads=[tb], writes=[tb])
        dma("sync", wdb_d[j], buf, reads=[tb], writes=[T("wdb_dram")])
    p.barrier()

    GT = 256
    SBK_LN = {0: (4, 5, 2), 1: (0, 1, 3)}

    def emit_ln(src_ap, total_t, dst_ap, dst_tok):
        xA = fa[:, 0:GT * 16].rearrange("p (c t) -> p c t", t=GT)
        for g in range(total_t // GT):
            u = g % 2
            t0 = g * GT
            sq = oba[:, u * 4096:u * 4096 + 4096].rearrange(
                "p (c t) -> p c t", t=GT)
            xnb = oba[:, 8192 + u * 4096:12288 + u * 4096].rearrange(
                "p (c t) -> p c t", t=GT)
            ro = 1024 * u
            m_r = rows[0:1, ro + 0:ro + 256]
            var = rows[0:1, ro + 256:ro + 512]
            rstd = rows[0:1, ro + 512:ro + 768]
            mrs = rows[0:1, ro + 768:ro + 1024]
            bcr = rows[0:1, ro + 512:ro + 1024]
            msq = rows[0:1, 2048 + 256 * u:2304 + 256 * u]
            lnv = rows[0:1, 2560 + 256 * u:2816 + 256 * u]
            bs, bq, bc = SBK_LN[u]
            for d4 in range(4):
                dma("sync", xA[:, 4 * d4:4 * d4 + 4, :],
                    src_ap[4 * d4:4 * d4 + 4, :, t0:t0 + GT]
                    .rearrange("c p t -> p c t"),
                    writes=[T(f"xa{dc}") for dc in range(4 * d4, 4 * d4 + 4)])
            for dc in range(16):
                p.ins("tensor", lambda e, r=xA[:, dc, :], s=(dc == 0),
                      t=(dc == 15), b=bs:
                      e.matmul(PS[b][0:1, 0:GT], onescol_f, r,
                               start=s, stop=t),
                      reads=[T(f"xa{dc}"), T("ones")], writes=[T(f"ps{bs}")])
            p.ins("vector", lambda e, o=oba[:, u * 4096:u * 4096 + 4096],
                  i=fa[:, 0:4096]: e.tensor_tensor(o, i, i, ALU.mult),
                  reads=[T(f"xa{dc}") for dc in range(16)],
                  writes=[T(f"sq{u}")])
            for dc in range(16):
                p.ins("tensor", lambda e, r=sq[:, dc, :], s=(dc == 0),
                      t=(dc == 15), b=bq:
                      e.matmul(PS[b][0:1, 0:GT], onescol[:], r,
                               start=s, stop=t),
                      reads=[T(f"sq{u}"), T("ones")], writes=[T(f"ps{bq}")])
            p.ins("scalar", lambda e, b=bs, o=m_r:
                  e.activation(o, PS[b][0:1, 0:GT], AF.Copy, scale=1.0 / DM),
                  reads=[T(f"ps{bs}")], writes=[T(f"lnm{u}")])
            p.ins("scalar", lambda e, b=bs, o=msq:
                  e.activation(o, PS[b][0:1, 0:GT], AF.Square, scale=1.0 / DM),
                  reads=[T(f"ps{bs}")], writes=[T(f"lnmsq{u}")])
            p.ins("vector", lambda e, b=bq, o=var, m2=msq:
                  e.scalar_tensor_tensor(o, PS[b][0:1, 0:GT], 1.0 / DM, m2,
                                         ALU.mult, ALU.subtract),
                  reads=[T(f"ps{bq}"), T(f"lnmsq{u}")], writes=[T(f"lnvar{u}")])
            p.ins("scalar", lambda e, o=lnv, i=var:
                  e.activation(o, i, AF.Ln, bias=cbeps),
                  reads=[T(f"lnvar{u}"), T("ones")], writes=[T(f"lnlnv{u}")])
            p.fence("scalar")
            p.ins("scalar", lambda e, o=rstd, i=lnv:
                  e.activation(o, i, AF.Exp, scale=-0.5),
                  reads=[T(f"lnlnv{u}")], writes=[T(f"lnrstd{u}")])
            p.ins("vector", lambda e, o=mrs, i=m_r, r=rstd:
                  e.scalar_tensor_tensor(o, i, -1.0, r, ALU.mult, ALU.mult),
                  reads=[T(f"lnm{u}"), T(f"lnrstd{u}")], writes=[T(f"lnmrs{u}")])
            p.ins("tensor", lambda e, b=bc, r=bcr:
                  e.matmul(PS[b][:, 0:512], onesrow_f, r, start=True, stop=True),
                  reads=[T(f"lnrstd{u}"), T(f"lnmrs{u}"), T("ones")],
                  writes=[T(f"ps{bc}")])
            for dc in range(16):
                tb = rbuf[:, (dc % 2) * GT:(dc % 2) * GT + GT]
                p.ins("vector", lambda e, o=tb, i=xA[:, dc, :], b=bc:
                      e.tensor_tensor(o, i, PS[b][:, 0:GT], ALU.mult),
                      reads=[T(f"xa{dc}"), T(f"ps{bc}")],
                      writes=[T(f"lntmp{dc % 2}")])
                p.ins("vector", lambda e, o=xnb[:, dc, :], i=tb, b=bc:
                      e.tensor_tensor(o, i, PS[b][:, GT:2 * GT], ALU.add),
                      reads=[T(f"lntmp{dc % 2}"), T(f"ps{bc}")],
                      writes=[T(f"xnb{u}_{dc}")])
                if dc % 4 == 3:
                    d0 = dc - 3
                    dma("gpsimd", dst_ap[d0:d0 + 4, :, t0:t0 + GT]
                        .rearrange("c p t -> p c t"),
                        xnb[:, d0:d0 + 4, :],
                        reads=[T(f"xnb{u}_{d}") for d in range(d0, d0 + 4)],
                        writes=[T(dst_tok)])

    emit_ln(xf, S, xnf_d, "xnf")
    p.barrier()
    xov = xo.rearrange("(c p) t -> c p t", p=128)
    emit_ln(xov, OWN, xno_d, "xno")
    p.barrier()

    wsb_v = wsb[:].rearrange("p (c n) -> p c n", n=1024)

    def load_w(w_ap, ch):
        for dc in range(16):
            p.ins("gpsimd", lambda e, o=wsb_v[:, dc, :],
                  i=w_ap[128 * dc:128 * (dc + 1), 1024 * ch:1024 * (ch + 1)]:
                  e.dma_start(out=o, in_=i),
                  reads=[T("wsb")], writes=[T("wsb")])

    xnc = xna[:].rearrange("p (u c n) -> p u c n", u=2, n=512)

    def load_xn(src_ap, buf, t0):
        for dc in range(16):
            dma("sync", xnc[:, buf, dc, :], src_ap[dc, :, t0:t0 + 512],
                writes=[T(f"xnc{buf}")])

    for ch in range(2):
        load_w(wk, ch)
        for tch in range(8):
            bu = tch % 2
            load_xn(xnf_d, bu, tch * 512)
            for ddl in range(8):
                dd = ch * 8 + ddl
                bk = ddl % 2
                for dc in range(16):
                    p.ins("tensor", lambda e, b=PS[bk],
                          l=wsb_v[:, dc, 128 * ddl:128 * (ddl + 1)],
                          r=xnc[:, bu, dc, :], s=(dc == 0), t=(dc == 15):
                          e.matmul(b, l, r, start=s, stop=t),
                          reads=[T("wsb"), T(f"xnc{bu}")], writes=[T(f"ps{bk}")])
                stg = kst[:, bk * 512:bk * 512 + 512]
                p.ins("vector", lambda e, o=stg, b=PS[bk]: e.tensor_copy(o, b),
                      reads=[T(f"ps{bk}")], writes=[T(f"kst{bk}")])
                dma("sync", k_d[dd, :, tch * 512:tch * 512 + 512], stg,
                    reads=[T(f"kst{bk}")], writes=[T("k_dram")])
        p.barrier()

    for ch in range(2):
        load_w(wv, ch)
        for tch in range(8):
            bu = tch % 2
            load_xn(xnf_d, bu, tch * 512)
            for ttl in range(4):
                tt = tch * 4 + ttl
                for dvl in range(2):
                    dvc = ch * 2 + dvl
                    bk = (ttl * 2 + dvl) % 2
                    for dc in range(16):
                        p.ins("tensor", lambda e, b=PS[bk],
                              l=xnc[:, bu, dc, 128 * ttl:128 * (ttl + 1)],
                              r=wsb_v[:, dc, 512 * dvl:512 * (dvl + 1)],
                              s=(dc == 0), t=(dc == 15):
                              e.matmul(b, l, r, start=s, stop=t),
                              reads=[T("wsb"), T(f"xnc{bu}")],
                              writes=[T(f"ps{bk}")])
                    stg = kst[:, bk * 512:bk * 512 + 512]
                    p.ins("vector", lambda e, o=stg, b=PS[bk]:
                          e.tensor_copy(o, b),
                          reads=[T(f"ps{bk}")], writes=[T(f"kst{bk}")])
                    for hh in range(4):
                        dma("sync", v_d[dvc * 4 + hh, tt],
                            stg[:, 128 * hh:128 * (hh + 1)],
                            reads=[T(f"kst{bk}")], writes=[T("v_dram")])
        p.barrier()

    for ch in range(2):
        load_w(wq, ch)
        for och in range(4):
            bu = och % 2
            load_xn(xno_d, bu, och * 512)
            for ddl in range(8):
                dd = ch * 8 + ddl
                bk = ddl % 2
                for dc in range(16):
                    p.ins("tensor", lambda e, b=PS[bk],
                          l=wsb_v[:, dc, 128 * ddl:128 * (ddl + 1)],
                          r=xnc[:, bu, dc, :], s=(dc == 0), t=(dc == 15):
                          e.matmul(b, l, r, start=s, stop=t),
                          reads=[T("wsb"), T(f"xnc{bu}")], writes=[T(f"ps{bk}")])
                stg = kst[:, bk * 512:bk * 512 + 512]
                p.ins("scalar", lambda e, o=stg, b=PS[bk]:
                      e.activation(o, b, AF.Copy, scale=QSCALE),
                      reads=[T(f"ps{bk}")], writes=[T(f"kst{bk}")])
                dma("sync", q_d[dd, :, och * 512:och * 512 + 512], stg,
                    reads=[T(f"kst{bk}")], writes=[T("q_dram")])
        p.barrier()

    kb = oba[:, 0:4096]
    vb = oba[:, 4096:8192]
    qb = oba[:, 8192:10240]
    sqq = oba[:, 10240:12288]
    sqk = oba[:, 12288:16384]
    kgf = fa[:, 0:64]

    def emit_sumpv(kt, nkt):
        bk = kt % 2
        pbuf = pb[:, 512 * bk:512 * (bk + 1)]
        p.ins("tensor", lambda e, b=PS[3][0:1, :], r=pbuf, s=(kt == 0),
              t=(kt == nkt - 1): e.matmul(b, onescol[:], r, start=s, stop=t),
              reads=[T(f"pb{bk}"), T("ones")], writes=[T("ps3")])
        p.ins("tensor", lambda e, l=vb[:, 128 * kt:128 * (kt + 1)],
              r=pbuf, s=(kt == 0), t=(kt == nkt - 1):
              e.matmul(PS[2][:], l, r, start=s, stop=t),
              reads=[T(f"pb{bk}"), T("vb")], writes=[T("ps2")])

    for h in range(H):
        dma("sync", kb, k_d[h], writes=[T("kb")])
        dma("sync", qb, q_d[h], writes=[T("qb")])
        for tt in range(32):
            dma("sync", vb[:, 128 * tt:128 * (tt + 1)], v_d[h, tt],
                writes=[T("vb")])
        p.ins("vector", lambda e: e.tensor_tensor(sqk, kb, kb, ALU.mult),
              reads=[T("kb")], writes=[T("sqk")])
        p.ins("vector", lambda e: e.tensor_tensor(sqq, qb, qb, ALU.mult),
              reads=[T("qb")], writes=[T("sqq")])
        p.ins("vector", lambda e: e.tensor_reduce(kgf,
              kb.rearrange("p (n w) -> p n w", w=64), AX.X, ALU.add),
              reads=[T("kb")], writes=[T("kgf")])
        p.fence("vector")
        p.ins("vector", lambda e: e.tensor_scalar(kgbt[:], kgf, 1.0 / 64, None,
              ALU.mult), reads=[T("kgf")], writes=[T("kgb")])
        for kc in range(8):
            bk = 4 + (kc % 2)
            p.ins("tensor", lambda e, b=PS[bk][0:1, :],
                  r=sqk[:, 512 * kc:512 * (kc + 1)]:
                  e.matmul(b, onescol[:], r, start=True, stop=True),
                  reads=[T("sqk"), T("ones")], writes=[T(f"ps{bk}")])
            p.ins("vector", lambda e, o=rows[0:1, 1024 + kc:1025 + kc],
                  b=PS[bk][0:1, :]: e.tensor_reduce(o, b, AX.X, ALU.max),
                  reads=[T(f"ps{bk}")], writes=[T("kmax8")])
        p.fence("vector")
        p.ins("vector", lambda e: e.tensor_reduce(rows[0:1, 1032:1033],
              rows[0:1, 1024:1032], AX.X, ALU.max),
              reads=[T("kmax8")], writes=[T("kmax")])
        p.fence("vector")
        p.ins("tensor", lambda e: e.matmul(PS[5][:, 0:1], onesrow_f,
              rows[0:1, 1032:1033], start=True, stop=True),
              reads=[T("kmax"), T("ones")], writes=[T("ps5")])
        p.ins("vector", lambda e: e.tensor_copy(bcols[:, 48:49], PS[5][:, 0:1]),
              reads=[T("ps5")], writes=[T("kmaxc")])
        for i in range(16):
            p.ins("tensor", lambda e, o=PS[5][:, 32 + i:33 + i],
                  l=sqq[:, 128 * i:128 * (i + 1)]:
                  e.matmul(o, l, onescol[:], start=True, stop=True),
                  reads=[T("sqq"), T("ones")], writes=[T("ps5")])
        p.ins("vector", lambda e: e.tensor_scalar(bcols[:, 0:16],
              PS[5][:, 32:48], bcols[:, 48:49], None, ALU.mult),
              reads=[T("ps5"), T("kmaxc")], writes=[T("bsq")])
        p.ins("scalar", lambda e: e.activation(bcols[:, 16:32], bcols[:, 0:16],
              AF.Ln, bias=cbtiny), reads=[T("bsq"), T("ones")], writes=[T("blog")])
        p.fence("scalar")
        p.ins("scalar", lambda e: e.activation(bcols[:, 0:16], bcols[:, 16:32],
              AF.Exp, scale=0.5), reads=[T("blog")], writes=[T("bcol")])
        p.ins("vector", lambda e: e.tensor_scalar(bcols[:, 16:32],
              bcols[:, 0:16], -1.02, 60.0, ALU.mult, ALU.add),
              reads=[T("bcol")], writes=[T("ccol")])
        p.fence("vector")
        p.ins("vector", lambda e: e.tensor_scalar(bcols[:, 16:32],
              bcols[:, 16:32], 0.0, None, ALU.min),
              reads=[T("ccol")], writes=[T("ccol")])
        p.fence("vector")
        for grp in range(2):
            for r in range(8):
                i = grp * 8 + r
                p.ins("tensor", lambda e, o=PS[6][:, 64 * r:64 * (r + 1)],
                      l=qb[:, 128 * i:128 * (i + 1)]:
                      e.matmul(o, l, kgbt[:], start=True, stop=True),
                      reads=[T("qb"), T("kgb")], writes=[T("ps6")])
            sl = slice(512 * grp, 512 * (grp + 1))
            p.ins("vector", lambda e, s=sl: e.tensor_tensor(g2a[:, s],
                  PS[6][:], npval[:, s], ALU.min),
                  reads=[T("ps6"), T("npval")], writes=[T("g2")])
            p.ins("vector", lambda e, s=sl: e.tensor_tensor(sela[:, s],
                  g2a[:, s], selfval[:, s], ALU.max),
                  reads=[T("g2"), T("selfval")], writes=[T("sela")])
        for i in range(16):
            p.ins("vector", lambda e, o=t8[:, 8 * i:8 * (i + 1)],
                  i_=sela[:, 64 * i:64 * (i + 1)]: e.max(o, i_),
                  reads=[T("sela")], writes=[T("t8")])
        p.fence("vector")
        for i in range(16):
            p.ins("vector", lambda e, o=g2a[:, 64 * i:64 * (i + 1)],
                  i_=sela[:, 64 * i:64 * (i + 1)], th=t8[:, 8 * i + 7:8 * i + 8]:
                  e.tensor_scalar(o, i_, th, None, ALU.is_ge),
                  reads=[T("sela"), T("t8")], writes=[T("g2")])
        for i in range(16):
            p.ins("vector", lambda e, o=bqa[:, 64 * i:64 * (i + 1)],
                  i_=g2a[:, 64 * i:64 * (i + 1)], c=bcols[:, 16 + i:17 + i]:
                  e.tensor_scalar(o, i_, c, None, ALU.mult),
                  reads=[T("g2"), T("ccol")], writes=[T("bq")])
            p.ins("vector", lambda e, o=sela[:, 64 * i:64 * (i + 1)],
                  i_=g2a[:, 64 * i:64 * (i + 1)]:
                  e.tensor_scalar(o, i_, 1e30, NEG, ALU.mult, ALU.add),
                  reads=[T("g2")], writes=[T("sela")])
        p.ins("vector", lambda e: e.tensor_tensor(g2a[:], sela[:], npbias[:],
              ALU.min), reads=[T("sela"), T("npbias")], writes=[T("g2")])
        p.ins("vector", lambda e: e.tensor_tensor(bqb[:], bqa[:], g2a[:],
              ALU.add), reads=[T("bq"), T("g2")], writes=[T("bqb")])
        for i in range(16):
            p.ins("tensor", lambda e, i_=bqb[:, 64 * i:64 * (i + 1)]:
                  e.transpose(PST[0:64, 0:128], i_, id128[:]),
                  reads=[T("bqb"), T("id128")], writes=[T("ps7")])
            p.ins("vector", lambda e, o=biasT[:, 128 * i:128 * (i + 1)]:
                  e.tensor_copy(o, PST[0:64, 0:128]),
                  reads=[T("ps7")], writes=[T("biasT")])
        if "dbg" in debug:
            if h == 1:
                dma("sync", id_dbg, id128[:], reads=[T("id128")],
                    writes=[T("dbgw")])
                dma("sync", em_dbg, emat[:], reads=[T("emat")],
                    writes=[T("dbgw")])
            dma("sync", bt_dbg[h], biasT[:], reads=[T("biasT")],
                writes=[T("dbgw")])
            dma("sync", g2_dbg[h], g2a[:], reads=[T("g2")], writes=[T("dbgw")])
            dma("sync", bq_dbg[h], bqb[:], reads=[T("bqb")], writes=[T("dbgw")])
            dma("sync", t8_dbg[h], t8[:], reads=[T("t8")], writes=[T("dbgw")])
            dma("sync", bqa_dbg[h], bqa[:], reads=[T("bq")], writes=[T("dbgw")])
        for qc in range(4):
            nkt = 8 * qc + 8
            qsl = slice(512 * qc, 512 * (qc + 1))
            prev = None
            for kt in range(nkt):
                bk = kt % 2
                p.ins("tensor", lambda e, b=PS[bk],
                      l=emat[:, 128 * kt:128 * (kt + 1)], r=biasT[:, qsl]:
                      e.matmul(b, l, r, start=True, stop=False),
                      reads=[T("emat"), T("biasT")], writes=[T(f"ps{bk}")])
                rel = kt - 8 * qc
                if 0 <= rel < 8 and rel % 2 == 0:
                    r_ = rel // 2
                    p.ins("tensor", lambda e, b=PS[bk],
                          r=mA[:, 512 * r_:512 * (r_ + 1)]:
                          e.matmul(b, id128[:], r, start=False, stop=False),
                          reads=[T("mA"), T("id128")], writes=[T(f"ps{bk}")])
                if 1 <= rel < 9 and (rel - 1) % 2 == 0:
                    r_ = (rel - 1) // 2
                    p.ins("tensor", lambda e, b=PS[bk],
                          r=mB[:, 512 * r_:512 * (r_ + 1)]:
                          e.matmul(b, id128[:], r, start=False, stop=False),
                          reads=[T("mB"), T("id128")], writes=[T(f"ps{bk}")])
                p.ins("tensor", lambda e, b=PS[bk],
                      l=kb[:, 128 * kt:128 * (kt + 1)], r=qb[:, qsl]:
                      e.matmul(b, l, r, start=False, stop=True),
                      reads=[T("kb"), T("qb")], writes=[T(f"ps{bk}")])
                if prev is not None:
                    emit_sumpv(prev, nkt)
                p.ins("scalar", lambda e, o=pb[:, 512 * bk:512 * (bk + 1)],
                      b=PS[bk]: e.activation(o, b, AF.Exp),
                      reads=[T(f"ps{bk}")], writes=[T(f"pb{bk}")])
                prev = kt
            emit_sumpv(prev, nkt)
            p.ins("vector", lambda e: e.reciprocal(rows[0:1, 0:512],
                  PS[3][0:1, :]), reads=[T("ps3")], writes=[T("rsum")])
            p.ins("tensor", lambda e: e.matmul(PS[4][:], onesrow_f,
                  rows[0:1, 0:512], start=True, stop=True),
                  reads=[T("rsum"), T("ones")], writes=[T("ps4")])
            p.ins("vector", lambda e: e.tensor_copy(rbuf[:], PS[4][:]),
                  reads=[T("ps4")], writes=[T("rbuf")])
            ob = obuf[:, 512 * (qc % 2):512 * (qc % 2) + 512]
            p.ins("vector", lambda e, o=ob: e.tensor_tensor(o, PS[2][:],
                  rbuf[:], ALU.mult),
                  reads=[T("ps2"), T("rbuf")], writes=[T(f"ob{qc % 2}")])
            dma("sync", o_d[h, :, 512 * qc:512 * (qc + 1)], ob,
                reads=[T(f"ob{qc % 2}")], writes=[T("o_dram")])
    p.barrier()

    och_v = xna[:].rearrange("p (c n) -> p c n", n=1024)
    for ch in range(2):
        load_w(wo, ch)
        for tch in range(4):
            bu = tch % 2
            for dc in range(16):
                dma("sync", och_v[:, dc, 512 * bu:512 * bu + 512],
                    o_d[dc, :, tch * 512:tch * 512 + 512],
                    writes=[T(f"oall{bu}")])
            for dol in range(8):
                dot = ch * 8 + dol
                bk = dol % 2
                for dc in range(16):
                    p.ins("tensor", lambda e, b=PS[bk],
                          l=wsb_v[:, dc, 128 * dol:128 * (dol + 1)],
                          r=och_v[:, dc, 512 * bu:512 * bu + 512],
                          s=(dc == 0), t=(dc == 15):
                          e.matmul(b, l, r, start=s, stop=t),
                          reads=[T("wsb"), T(f"oall{bu}")],
                          writes=[T(f"ps{bk}")])
                stg = kst[:, bk * 512:bk * 512 + 512]
                p.ins("vector", lambda e, o=stg, b=PS[bk]: e.tensor_copy(o, b),
                      reads=[T(f"ps{bk}")], writes=[T(f"kst{bk}")])
                dma("sync", s_d[dot, :, tch * 512:tch * 512 + 512], stg,
                    reads=[T(f"kst{bk}")], writes=[T("s_dram")])
        p.barrier()

    wgj = oba[:, 0:4096]
    wuj = oba[:, 4096:8192]
    actb_a = oba[:, 8192:10240]
    wdj8 = oba[:, 6144:10240]

    def actR(j):
        if j < 32:
            return xna[:, 512 * j:512 * j + 512]
        return oba[:, 512 * (j - 32):512 * (j - 32) + 512]
    stj4 = oba[:, 10240:12288]
    xt4 = fa[:, 0:2048]
    sgt = fa[:, 0:1024]
    xno2 = xna[:].rearrange("p (c n) -> p c n", n=1024)

    for tc in range(2):
        for dc in range(16):
            dma("sync", xno2[:, dc, :],
                xno_d[dc, :, tc * 1024:tc * 1024 + 1024], writes=[T("xno2")])
        for half in range(2):
            bk = 4 + half
            for dc in range(16):
                p.ins("tensor", lambda e, b=PS[bk][0:1, :],
                      r=xno2[:, dc, 512 * half:512 * half + 512],
                      s=(dc == 0), t=(dc == 15):
                      e.matmul(b, onescol[:], r, start=s, stop=t),
                      reads=[T("xno2"), T("ones")], writes=[T(f"ps{bk}")])
            p.ins("vector", lambda e, o=rows_b[0:1, 512 * half:512 * half + 512],
                  b=PS[bk][0:1, :]: e.tensor_copy(o, b),
                  reads=[T(f"ps{bk}")], writes=[T("crow")])
        for j in range(NHT):
            bu = j % 2
            wgv = wgj[:, 2048 * bu:2048 * bu + 2048].rearrange(
                "p (c n) -> p c n", n=128)
            wuv = wuj[:, 2048 * bu:2048 * bu + 2048].rearrange(
                "p (c n) -> p c n", n=128)
            actb = actb_a[:, 1024 * bu:1024 * bu + 1024]
            p.ins("gpsimd", lambda e, o=wgj[:, 2048 * bu:2048 * bu + 2048],
                  i=wgA[j]: e.dma_start(out=o, in_=i),
                  reads=[T(f"wg{bu}")], writes=[T(f"wg{bu}")])
            p.ins("gpsimd", lambda e, o=wuj[:, 2048 * bu:2048 * bu + 2048],
                  i=wuA[j]: e.dma_start(out=o, in_=i),
                  reads=[T(f"wu{bu}")], writes=[T(f"wu{bu}")])
            for half in range(2):
                hsl = slice(512 * half, 512 * half + 512)
                gb = PS[half]
                ub = PS[2 + half]
                p.ins("tensor", lambda e, b=gb, r=rows_b[0:1, hsl]:
                      e.matmul(b, m128row, r, start=True, stop=False),
                      reads=[T("crow"), T("ones")], writes=[T(f"ps{half}")])
                for dc in range(16):
                    p.ins("tensor", lambda e, b=gb, l=wgv[:, dc, :],
                          r=xno2[:, dc, hsl], t=(dc == 15):
                          e.matmul(b, l, r, start=False, stop=t),
                          reads=[T(f"wg{bu}"), T("xno2")],
                          writes=[T(f"ps{half}")])
                p.ins("tensor", lambda e, b=ub, r=rows_b[0:1, hsl]:
                      e.matmul(b, m128row, r, start=True, stop=False),
                      reads=[T("crow"), T("ones")], writes=[T(f"ps{2 + half}")])
                for dc in range(16):
                    p.ins("tensor", lambda e, b=ub, l=wuv[:, dc, :],
                          r=xno2[:, dc, hsl], t=(dc == 15):
                          e.matmul(b, l, r, start=False, stop=t),
                          reads=[T(f"wu{bu}"), T("xno2")],
                          writes=[T(f"ps{2 + half}")])
                sgv = sgt[:, 512 * half:512 * half + 512]
                p.ins("scalar", lambda e, o=sgv, b=gb, sc=sgc[:, j:j + 1]:
                      e.activation(o, b, AF.Silu, scale=sc),
                      reads=[T(f"ps{half}"), T("sgc")], writes=[T(f"sg{half}")])
                p.ins("vector", lambda e, o=actb[:, hsl], b=ub,
                      sc=suc[:, j:j + 1], s2=sgv:
                      e.scalar_tensor_tensor(o, b, sc, s2, ALU.mult, ALU.mult),
                      reads=[T(f"ps{2 + half}"), T("suc"), T(f"sg{half}")],
                      writes=[T(f"actb{bu}")])
                p.ins("tensor", lambda e, b=PS[4 + half][0:1, :],
                      r=actb[:, hsl], s=(j == 0), t=(j == NHT - 1):
                      e.matmul(b, onescol[:], r, start=s, stop=t),
                      reads=[T(f"actb{bu}"), T("ones")],
                      writes=[T(f"ps{4 + half}")])
            dma("sync", act_d[j], actb, reads=[T(f"actb{bu}")],
                writes=[T("act_dram")])
        p.ins("tensor", lambda e: e.matmul(PS[6][0:1, 0:1], onescol, onescol,
              start=True, stop=True), reads=[T("ones")], writes=[T("gu_drain")])
        for half in range(2):
            p.ins("vector", lambda e, o=rows_b[0:1, 512 * half:512 * half + 512],
                  b=PS[4 + half][0:1, :]: e.tensor_copy(o, b),
                  reads=[T(f"ps{4 + half}"), T("gu_drain")],
                  writes=[T("carow")])
        p.barrier()
        p.ins("scalar", lambda e: e.activation(rows[0:1, 0:1], rows[0:1, 1:2],
              AF.Copy), writes=[T("sdrain")])
        for tsub in range(2):
            tpos = tc * 1024 + 512 * tsub
            for j in range(NHT):
                g = min(j + 6, NHT - 1)
                dma("sync" if j % 2 == 0 else "gpsimd", actR(j),
                    act_d[j, :, 512 * tsub:512 * tsub + 512],
                    reads=[T(f"actR{j}"), T(f"dnmm{g}")],
                    writes=[T(f"actR{j}")])
            for dg in range(4):
                for u in range(4):
                    dot = dg * 4 + u
                    dma("sync", stj4[:, 512 * u:512 * u + 512],
                        s_d[dot, :, tpos:tpos + 512],
                        reads=[T(f"stj{u}")], writes=[T(f"stj{u}")])
                    dma("gpsimd", xt4[:, 512 * u:512 * u + 512],
                        xo[128 * dot:128 * (dot + 1), tpos:tpos + 512],
                        reads=[T(f"xtj{u}")], writes=[T(f"xtj{u}")])
                for u in range(4):
                    p.ins("tensor", lambda e, b=PS[u],
                          r=rows_b[0:1, 512 * tsub:512 * tsub + 512]:
                          e.matmul(b, m128row, r, start=True, stop=False),
                          reads=[T("carow"), T("ones"), T("sdrain")],
                          writes=[T(f"ps{u}")])
                PF = 6

                def issue_wd(j, dg=dg):
                    bu = j % 8
                    dma("gpsimd", wdj8[:, 512 * bu:512 * bu + 512],
                        wdb_d[j, :, 512 * dg:512 * dg + 512],
                        reads=[T(f"wdj{bu}")], writes=[T(f"wdj{bu}")])
                for j in range(PF):
                    issue_wd(j)
                for j in range(NHT):
                    bu = j % 8
                    aj = actR(j)
                    wj = wdj8[:, 512 * bu:512 * bu + 512]
                    gw = T(f"wdj{min(j + 4, NHT - 1) % 8}")
                    ga_j = j + 6 if j + 6 < NHT else NHT - 2 + (j % 2)
                    ga = T(f"actR{ga_j}")
                    for u in range(4):
                        wrs = [T(f"ps{u}")]
                        if u == 3:
                            wrs.append(T(f"dnmm{j}"))
                        p.ins("tensor", lambda e, b=PS[u],
                              l=wj[:, 128 * u:128 * (u + 1)], r=aj,
                              t=(j == NHT - 1):
                              e.matmul(b, l, r, start=False, stop=t),
                              reads=[T(f"wdj{bu}"), T(f"actR{j}"), gw, ga],
                              writes=wrs)
                    if j + PF < NHT:
                        bu2 = (j + PF) % 8
                        dma("gpsimd", wdj8[:, 512 * bu2:512 * bu2 + 512],
                            wdb_d[j + PF, :, 512 * dg:512 * dg + 512],
                            reads=[T(f"wdj{bu2}"), T(f"dnmm{j}")],
                            writes=[T(f"wdj{bu2}")])
                p.ins("tensor", lambda e: e.matmul(PS[4][0:1, 0:1],
                      onescol, onescol, start=True, stop=True),
                      reads=[T("ones")], writes=[T("dn_drain")])
                for u in range(4):
                    dot = dg * 4 + u
                    par = u
                    gt = gtj[:, 512 * u:512 * u + 512]
                    st_ = stj4[:, 512 * u:512 * u + 512]
                    xt_ = xt4[:, 512 * u:512 * u + 512]
                    ot = outt[:, 512 * u:512 * u + 512]
                    stg_d = fa[:, 2048 + 512 * u:2560 + 512 * u]
                    p.ins("vector", lambda e, o=stg_d, b=PS[u]:
                          e.tensor_copy(o, b),
                          reads=[T(f"ps{u}"), T("dn_drain")],
                          writes=[T(f"dstg{u}")])
                    p.ins("scalar", lambda e, o=gt, i=stg_d,
                          sc=sdc[:, dot:dot + 1]:
                          e.activation(o, i, AF.Silu, scale=sc),
                          reads=[T(f"dstg{u}"), T("sdc")], writes=[T(f"gt{par}")])
                    p.ins("vector", lambda e, o=gt, s_=st_:
                          e.tensor_tensor(o, o, s_, ALU.mult),
                          reads=[T(f"gt{par}"), T(f"stj{u}")],
                          writes=[T(f"gt{par}")])
                    p.ins("vector", lambda e, o=ot, g=gt, x=xt_:
                          e.tensor_tensor(o, g, x, ALU.add),
                          reads=[T(f"gt{par}"), T(f"xtj{u}"),
                                 T(f"outd{min(u + 2, 3)}")],
                          writes=[T(f"outt{par}")])
                    dma("sync",
                        out_d[128 * dot:128 * (dot + 1), tpos:tpos + 512], ot,
                        reads=[T(f"outt{par}")], writes=[T(f"outd{u}")])
        p.barrier()

    p.finish()
    p.emit()
    return nc, stack


_CACHE = {}


def _get_prog():
    if "nc" not in _CACHE:
        nc, stack = build()
        _CACHE["nc"] = nc
        _CACHE["stack"] = stack
    return _CACHE["nc"]


def _host_masks(parity):
    g = np.array([2 * i + parity for i in range(16)])
    P = g[:, None] * 128 + np.arange(128)[None, :]
    n = np.arange(64)[None, None, :]
    selfblk = (P[..., None] // 64)
    selfval = np.where(n == selfblk, 1e30, NEG)
    npval = np.where(P[..., None] < (n + 1) * 64, NEG, 1e30)
    npbias = np.where(n > selfblk, NEG, 0.0)

    def pack(m):
        return np.ascontiguousarray(
            m.transpose(1, 0, 2).reshape(128, 1024)).astype(BF)

    tri = np.where(np.arange(128)[:, None] > np.arange(128)[None, :],
                   NEG, 0.0).astype(np.float32)
    full = np.full((128, 128), NEG, dtype=np.float32)
    zero = np.zeros((128, 128), dtype=np.float32)
    tA = tri if parity == 0 else zero
    tB = full if parity == 0 else tri
    mA_ = np.zeros((4, 128, 512), np.float32)
    mB_ = np.zeros((4, 128, 512), np.float32)
    for r in range(4):
        mA_[r, :, 128 * r:128 * (r + 1)] = tA
        mB_[r, :, 128 * r:128 * (r + 1)] = tB
    return (pack(selfval), pack(npval), pack(npbias),
            np.ascontiguousarray(mA_.transpose(1, 0, 2).reshape(128, 2048)
                                 .astype(BF)),
            np.ascontiguousarray(mB_.transpose(1, 0, 2).reshape(128, 2048)
                                 .astype(BF)))


def kernel(L0_attn_hidden_states, L0_attn_Wq, L0_attn_Wk, L0_attn_Wv,
           L0_attn_Wo, L0_ffn_wq_gate, L0_ffn_s_gate, L0_ffn_wq_up,
           L0_ffn_s_up, L0_ffn_wq_down, L0_ffn_s_down):
    x = np.asarray(L0_attn_hidden_states, dtype=np.float32)
    shared = {
        "wq": np.ascontiguousarray(np.asarray(L0_attn_Wq, np.float32)),
        "wk": np.ascontiguousarray(np.asarray(L0_attn_Wk, np.float32)),
        "wv": np.ascontiguousarray(np.asarray(L0_attn_Wv, np.float32)),
        "wo": np.ascontiguousarray(np.asarray(L0_attn_Wo, np.float32)),
        "wgA": np.ascontiguousarray(
            np.asarray(L0_ffn_wq_gate, np.int32).reshape(NHT, 128, 16, 128)
            .transpose(0, 3, 2, 1)),
        "wuA": np.ascontiguousarray(
            np.asarray(L0_ffn_wq_up, np.int32).reshape(NHT, 128, 16, 128)
            .transpose(0, 3, 2, 1)),
        "wdR": np.ascontiguousarray(
            np.asarray(L0_ffn_wq_down, np.int32).T.reshape(NHT, 128, 2048)),
        "sg": np.ascontiguousarray(
            np.asarray(L0_ffn_s_gate, np.float32).reshape(NHT, 128).T),
        "su": np.ascontiguousarray(
            np.asarray(L0_ffn_s_up, np.float32).reshape(NHT, 128).T),
        "sd": np.ascontiguousarray(
            np.asarray(L0_ffn_s_down, np.float32).reshape(16, 128).T),
        "emat": np.ascontiguousarray(
            (np.arange(64)[:, None] == (np.arange(S)[None, :] // 64))
            .astype(BF)),
        "id128": np.eye(128).astype(BF),
    }
    in_maps = []
    own_idx_all = []
    for core in range(8):
        b, parity = core // 2, core % 2
        xT = np.ascontiguousarray(x[b].T)
        own = np.concatenate([np.arange(128 * (2 * i + parity),
                                        128 * (2 * i + parity) + 128)
                              for i in range(16)])
        own_idx_all.append(own)
        sv, npv, npb, mA_, mB_ = _host_masks(parity)
        m = dict(shared)
        m["xf"] = np.ascontiguousarray(xT.reshape(16, 128, S))
        m["xo"] = np.ascontiguousarray(xT[:, own])
        m["selfval"] = sv
        m["npval"] = npv
        m["npbias"] = npb
        m["mA"] = mA_
        m["mB"] = mB_
        in_maps.append(m)

    nc = _get_prog()
    res = run_bass_kernel_spmd(nc, in_maps, core_ids=list(range(8)))
    out = np.empty((4, S, DM), dtype=np.float32)
    for core in range(8):
        b = core // 2
        out[b, own_idx_all[core], :] = res.results[core]["out"].T
    return out
